# version 02
# baseline (speedup 1.0000x reference)
import sys
sys.path.insert(0, '/opt/trn_rl_repo')
import numpy as np
import ml_dtypes
import concourse.bass as bass
import concourse.mybir as mybir
from contextlib import ExitStack

F32 = mybir.dt.float32
BF16 = mybir.dt.bfloat16
AF = mybir.ActivationFunctionType
ALU = mybir.AluOpType
AX = mybir.AxisListType

NCORES = 8
B, S, D, H, DH = 2, 4096, 2048, 16, 128
INNER = 4 * D
NT = 1024
NCH = D // 128
HCH = INNER // 128
RC = 1.5 * (2.0 ** 23)
EPS = 1e-5

bf16 = ml_dtypes.bfloat16



def host_prep(inp):
    C = np.ascontiguousarray
    sh = {}
    sh["l0_wqkv"] = C(inp["L0_attn_Wqkv"])
    sh["l0_bproj"] = C(inp["L0_attn_bproj"].reshape(D, 1))
    sh["l0_w1t"] = C(inp["L0_ffn_w1"].T)
    sh["l0_w2t"] = C(inp["L0_ffn_w2"].T)
    sh["l1_gatet"] = C(inp["L1_ffn_w_gate"].T)
    sh["l1_upt"] = C(inp["L1_ffn_w_up"].T)
    sh["l1_downt"] = C(inp["L1_ffn_w_down"].T)
    sh["ones_f"] = np.ones((128, 1), np.float32)
    sh["ones_r"] = np.ones((1, 128), np.float32)
    sh["ones_rb"] = np.ones((1, 128), bf16)
    sh["iden_b"] = np.eye(128).astype(bf16)
    bands = np.zeros((128, 4 * 512), np.float32)
    q = np.arange(512)[None, :]
    k = np.arange(128)[:, None]
    for j in range(4):
        bands[:, j * 512:(j + 1) * 512] = (q >= (128 * j + k)).astype(np.float32)
    sh["bands"] = bands.astype(bf16)

    percore = []
    x0 = np.asarray(inp["L0_attn_x"])
    wqkv = np.asarray(inp["L0_attn_Wqkv"])
    bqkv = np.asarray(inp["L0_attn_bqkv"])
    wproj = np.asarray(inp["L0_attn_Wproj"])
    wq, wk, wv, wo = (np.asarray(inp["L1_attn_W" + n]) for n in "qkvo")
    w1t, w2t = sh["l0_w1t"], sh["l0_w2t"]
    gt, ut, dt_ = sh["l1_gatet"], sh["l1_upt"], sh["l1_downt"]
    for c in range(NCORES):
        s, r = c // 4, c % 4
        pc = {}
        pc["x0"] = C(x0[s, 1024 * r:1024 * (r + 1), :].T)
        cols = slice(512 * r, 512 * (r + 1))
        pc["l0_wqkv_sl"] = C(np.concatenate(
            [wqkv[:, cols], wqkv[:, 2048:][:, cols], wqkv[:, 4096:][:, cols]], axis=1))
        pc["l0_bqkv_sl"] = C(np.concatenate(
            [bqkv[cols], bqkv[2048:][cols], bqkv[4096:][cols]]).reshape(1536, 1))
        pc["l0_wproj_sl"] = C(wproj[512 * r:512 * (r + 1), :])
        pc["l1_wq_sl"] = C(wq[:, cols])
        pc["l1_wk_sl"] = C(wk[:, cols])
        pc["l1_wv_sl"] = C(wv[:, cols])
        pc["l1_wo_sl"] = C(wo[512 * r:512 * (r + 1), :])
        pc["sh_l0w1"] = C(w1t[256 * c:256 * (c + 1)])
        pc["sh_l0w2"] = C(w2t[1024 * c:1024 * (c + 1)])
        pc["sh_l1g"] = C(gt[256 * c:256 * (c + 1)])
        pc["sh_l1u"] = C(ut[256 * c:256 * (c + 1)])
        pc["sh_l1d"] = C(dt_[1024 * c:1024 * (c + 1)])
        percore.append(pc)
    return sh, percore



class Ctx:
    def __init__(self, name, debug=False):
        self.nc = bass.Bass("TRN2", target_bir_lowering=False, debug=debug)
        self.st = ExitStack()
        self._sems = {}

    def sb(self, name, shape, dt=F32):
        return self.st.enter_context(self.nc.sbuf_tensor(name, shape, dt))

    def ps(self, name, shape, dt=F32):
        return self.st.enter_context(self.nc.psum_tensor(name, shape, dt))

    def sem(self, name):
        h = self.st.enter_context(self.nc.semaphore(name))
        self._sems[name] = h
        return h

    def block(self):
        return self.st.enter_context(self.nc.Block())

    def close(self):
        self.st.close()


class PDma:
    def __init__(self, c, name, nslots=2):
        self.sems = [c.sem(f"{name}{i}") for i in range(nslots)]
        self.n = nslots

    def inc(self, instr, i):
        instr.then_inc(self.sems[i % self.n], 16)

    def wait(self, eng, i):
        eng.wait_ge(self.sems[i % self.n], 16 * (i // self.n + 1))


def din(nc, name, shape, dt=F32):
    return nc.dram_tensor(name, shape, dt, kind="ExternalInput")


def dout(nc, name, shape, dt=F32):
    return nc.dram_tensor(name, shape, dt, kind="ExternalOutput")



def build_A(l, debug=False):
    c = Ctx(f"A{l}", debug=debug)
    nc = c.nc
    NMAT = 2 if l == 0 else 3
    SZW = D * INNER

    xT = din(nc, "xT", [D, NT])
    if l == 0:
        wdr = [din(nc, "w1t", [D, INNER]), din(nc, "w2t", [INNER, D])]
        shdr = [din(nc, "sh_w1", [D // 8, INNER]), din(nc, "sh_w2", [INNER // 8, D])]
    else:
        wdr = [din(nc, "gatet", [D, INNER]), din(nc, "upt", [D, INNER]),
               din(nc, "downt", [INNER, D])]
        shdr = [din(nc, "sh_g", [D // 8, INNER]), din(nc, "sh_u", [D // 8, INNER]),
                din(nc, "sh_d", [INNER // 8, D])]
    ones_f = din(nc, "ones_f", [128, 1])
    ones_r = din(nc, "ones_r", [1, 128])
    iden_f = din(nc, "iden_f", [128, 128])
    xn_o = dout(nc, "xn", [D, NT], BF16)
    ffn_o = dout(nc, "ffn", [D, NT])
    arin = nc.dram_tensor("arin", [1, NMAT], F32)
    arout = nc.dram_tensor("arout", [1, NMAT], F32)

    hscr = nc.dram_tensor("hscr", [INNER, NT], F32)
    t2scr = nc.dram_tensor("t2scr", [INNER, NT], BF16)
    xnf = nc.dram_tensor("xnf", [D, NT], F32)
    sxd = c.sb("sxd", [128, 2 * NT])
    sxnf = c.sb("sxnf", [128, 2 * NT])
    smx = c.sb("smx", [128, NT])
    sxn = c.sb("sxn", [128, NCH * NT], BF16)
    ssq = c.sb("ssq", [128, 4 * NT])
    srow = c.sb("srow", [1, 6 * NT])
    sbc = c.sb("sbc", [128, 2 * NT])
    smt = c.sb("smt", [128, NT])
    sacc = c.sb("sacc", [128, NT]) if l == 0 else None
    stok = c.sb("stok", [128, 8])
    sone_f = c.sb("sone_f", [128, 1])
    sone_r = c.sb("sone_r", [1, 128])
    siden = c.sb("siden", [128, 128])
    swsh = c.sb("swsh", [128, 2 * 1024])
    scol = c.sb("scol", [128, 96])
    srsc = c.sb("srsc", [1, 64])
    swsc = c.sb("swsc", [128, 2 * NMAT])
    swst = c.sb("swst", [128, 2 * 512])
    swq = c.sb("swq", [128, 2 * 512], BF16)
    swqs = c.sb("swqs", [128, 512])
    shs = c.sb("shs", [128, 8 * 512])
    sh2 = c.sb("sh2", [128, 2 * NT])
    st2r = c.sb("st2r", [128, 2 * NT], BF16)
    sti2 = c.sb("sti2", [128, 2 * NT], BF16)
    sgt = c.sb("sgt", [128, 8 * 512]) if l == 1 else None
    sfo = c.sb("sfo", [128, 8 * 512])

    psA = c.ps("psA", [128, 1024])
    psB = c.ps("psB", [128, 1024])
    psC = c.ps("psC", [128, 1024])
    psD = c.ps("psD", [128, 512])
    psE = c.ps("psE", [128, 512])
    PB8 = [psA[:, 0:512], psA[:, 512:1024], psB[:, 0:512], psB[:, 512:1024],
           psC[:, 0:512], psC[:, 512:1024], psD[:, :], psE[:, :]]

    RO = {"mean": 0, "a": 1 * NT, "b": 2 * NT, "tmp": 3 * NT, "tmp2": 4 * NT,
          "n2": 5 * NT, "rms": 0, "maxa": 1 * NT, "clp": 2 * NT, "g": 5 * NT,
          "d1": 0}

    def row(k, lo=0, hi=NT):
        return srow[0:1, RO[k] + lo: RO[k] + hi]

    BO = {"a": 0, "b": NT, "g": 0, "d": NT}

    sems, cnt = {}, {}
    for sn in ["s_x1", "s_x2", "s_k", "s_sq", "s_ma", "s_mm", "s_v", "s_sc",
               "s_vx", "s_sh", "s_pr", "s_a", "s_cc", "s_ar", "s_mt", "s_vt",
               "s_wl", "s_wq", "s_wu", "s_ev", "s_gl", "s_sq2", "s_mb",
               "s_hsq", "s_mh", "s_stx", "s_stf", "s_vf", "s_up", "s_sth",
               "s_pc", "s_va", "s_hr", "s_hr2", "s_tw", "s_u2", "s_vm",
               "s_stxa", "s_stxb"]:
        sems[sn] = c.sem(sn)
        cnt[sn] = 0

    px1 = PDma(c, "px1")
    px2 = PDma(c, "px2")
    pwl = PDma(c, "pwl")
    psh = PDma(c, "psh")
    phr = PDma(c, "phr")
    phr2 = PDma(c, "phr2")
    ptw = PDma(c, "ptw")
    pstf = PDma(c, "pstf", 4)
    pxr = PDma(c, "pxr")

    blk = c.block()

    def I(instr, s, n=1):
        cnt[s] += n
        instr.then_inc(sems[s], n)
        return cnt[s]

    def W(eng, s, v=None):
        eng.wait_ge(sems[s], cnt[s] if v is None else v)

    @blk.sync
    def _(sync):
        I(sync.dma_start(out=sone_f[:], in_=ones_f[:, :]), "s_k", 16)
        I(sync.dma_start(out=sone_r[:], in_=ones_r[:, :]), "s_k", 16)
        I(sync.dma_start(out=siden[:], in_=iden_f[:, :]), "s_k", 16)
        for j in range(NCH):
            if j >= 2:
                W(sync, "s_ma", j - 1)
            px1.inc(sync.dma_start(out=sxd[:, (j % 2) * NT:(j % 2 + 1) * NT],
                                   in_=xT[128 * j:128 * (j + 1), :]), j)
        for j in range(NCH):
            if j < 2:
                W(sync, "s_ma", 15 + j)
            else:
                W(sync, "s_vx", j - 1)
            px2.inc(sync.dma_start(out=sxd[:, (j % 2) * NT:(j % 2 + 1) * NT],
                                   in_=xT[128 * j:128 * (j + 1), :]), j)


    wtiles = []
    if l == 0:
        for hb in range(16):
            for ic in range(16):
                wtiles.append((0, ic, hb))
        for ob in range(4):
            for ic in range(64):
                wtiles.append((1, ic, ob))
    else:
        for hb in range(16):
            for ic in range(16):
                wtiles.append((0, ic, hb))
            for ic in range(16):
                wtiles.append((1, ic, hb))
        for ob in range(4):
            for ic in range(64):
                wtiles.append((2, ic, ob))
    NWT = len(wtiles)

    @blk.sync
    def _(sync):
        for n, (m, ic, cb) in enumerate(wtiles):
            if n >= 2:
                W(sync, "s_wq", n - 1)
            pwl.inc(sync.dma_start(
                out=swst[:, (n % 2) * 512:(n % 2 + 1) * 512],
                in_=wdr[m][128 * ic:128 * (ic + 1), 512 * cb:512 * (cb + 1)]), n)


    shard_tiles = []
    for m in range(NMAT):
        rsh, csh = shdr[m].shape
        shard_tiles += [(m, ri, ci) for ri in range(rsh // 128)
                        for ci in range(csh // 1024)]
    NSH = len(shard_tiles)
    tiles_per_mat = [sum(1 for t in shard_tiles if t[0] == mm) for mm in range(NMAT)]

    @blk.gpsimd
    def _(gpsimd):
        for j in range(NCH):
            W(gpsimd, "s_vx", j + 1)
            ssem = "s_stxa" if (j % 2) == 0 else "s_stxb"
            I(gpsimd.dma_start(out=xn_o[128 * j:128 * (j + 1), :],
                               in_=sxnf[:, (j % 2) * NT:(j % 2 + 1) * NT]), ssem, 16)
            I(gpsimd.dma_start(out=xnf[128 * j:128 * (j + 1), :],
                               in_=sxnf[:, (j % 2) * NT:(j % 2 + 1) * NT]), ssem, 16)
        pass


    @blk.scalar
    def _(scalar):
        for j in range(NCH):
            px1.wait(scalar, j)
            if j >= 2:
                W(scalar, "s_ma", j - 1)
            I(scalar.activation(ssq[:, (j % 2) * NT:(j % 2 + 1) * NT],
                                sxd[:, (j % 2) * NT:(j % 2 + 1) * NT], AF.Square),
              "s_sq", 1)

    @blk.tensor
    def _(tensor):
        W(tensor, "s_k", 48)
        for j in range(NCH):
            px1.wait(tensor, j)
            for hf in range(2):
                tensor.matmul(psA[0:1, hf * 512:(hf + 1) * 512], sone_f[:],
                              sxd[:, (j % 2) * NT + hf * 512:(j % 2) * NT + (hf + 1) * 512],
                              start=(j == 0), stop=(j == NCH - 1))
            W(tensor, "s_sq", j + 1)
            for hf in range(2):
                mm = tensor.matmul(psB[0:1, hf * 512:(hf + 1) * 512], sone_f[:],
                                   ssq[:, (j % 2) * NT + hf * 512:(j % 2) * NT + (hf + 1) * 512],
                                   start=(j == 0), stop=(j == NCH - 1))
            I(mm, "s_ma", 1)

    @blk.vector
    def _(vector):
        W(vector, "s_ma", NCH)
        vector.tensor_scalar(row("mean"), psA[0:1, :], 1.0 / D, 0.0, ALU.mult, ALU.add)
        vector.drain()
        vector.tensor_tensor(row("tmp"), row("mean"), row("mean"), ALU.mult)
        vector.drain()
        vector.scalar_tensor_tensor(row("tmp"), psB[0:1, :], 1.0 / D, row("tmp"),
                                    ALU.mult, ALU.subtract)
        vector.drain()
        I(vector.tensor_scalar(row("tmp"), row("tmp"), 1e-5, 0.0, ALU.add, ALU.add),
          "s_v", 1)

    @blk.scalar
    def _(scalar):
        W(scalar, "s_v", 1)
        I(scalar.activation(row("tmp2"), row("tmp"), AF.Sqrt), "s_sc", 1)

    @blk.vector
    def _(vector):
        W(vector, "s_sc", 1)
        vector.reciprocal(row("a"), row("tmp2"))
        vector.drain()
        vector.tensor_tensor(row("b"), row("mean"), row("a"), ALU.mult)
        vector.drain()
        I(vector.tensor_scalar(row("b"), row("b"), -1.0, 0.0, ALU.mult, ALU.add),
          "s_v", 1)

    @blk.tensor
    def _(tensor):
        W(tensor, "s_v", 2)
        for hf in range(2):
            tensor.matmul(psC[:, hf * 512:(hf + 1) * 512], sone_r[:],
                          row("a", hf * 512, (hf + 1) * 512), start=True, stop=True)
        for hf in range(2):
            mm = tensor.matmul(psB[:, hf * 512:(hf + 1) * 512], sone_r[:],
                               row("b", hf * 512, (hf + 1) * 512), start=True, stop=True)
        I(mm, "s_mm", 1)

    @blk.vector
    def _(vector):
        W(vector, "s_mm", 1)
        vector.tensor_copy(sbc[:, BO["a"]:BO["a"] + NT], psC[:, :])
        vector.drain()
        I(vector.tensor_copy(sbc[:, BO["b"]:BO["b"] + NT], psB[:, :]), "s_v", 1)

    MB0x = cnt["s_mb"]

    @blk.vector
    def _(vector):
        W(vector, "s_v", 3)
        for j in range(NCH):
            px2.wait(vector, j)
            if j >= 2:
                W(vector, "s_stxa" if (j % 2) == 0 else "s_stxb", 32 * ((j - 2) // 2 + 1))
                W(vector, "s_sq2", j - 1)
            vector.tensor_tensor(smt[:, :], sxd[:, (j % 2) * NT:(j % 2 + 1) * NT],
                                 sbc[:, BO["a"]:BO["a"] + NT], ALU.mult)
            vector.drain()
            I(vector.tensor_tensor(sxnf[:, (j % 2) * NT:(j % 2 + 1) * NT], smt[:, :],
                                   sbc[:, BO["b"]:BO["b"] + NT], ALU.add), "s_vx", 1)
            vector.drain()
            if j >= 1:
                W(vector, "s_sq2", j)
                if j == 1:
                    vector.tensor_copy(smx[:, :], ssq[:, 0:NT])
                else:
                    vector.tensor_tensor(smx[:, :], smx[:, :],
                                         ssq[:, ((j - 1) % 2) * NT:((j - 1) % 2 + 1) * NT],
                                         ALU.max)
                vector.drain()
                I(vector.nop(), "s_vm", 1)
        W(vector, "s_sq2", NCH)
        vector.tensor_tensor(smx[:, :], smx[:, :],
                             ssq[:, ((NCH - 1) % 2) * NT:((NCH - 1) % 2 + 1) * NT],
                             ALU.max)
        vector.drain()
        I(vector.nop(), "s_v", 1)

    @blk.scalar
    def _(scalar):
        for j in range(NCH):
            W(scalar, "s_vx", j + 1)
            if j >= 2:
                if l == 0:
                    W(scalar, "s_mb", MB0x + j - 1)
                W(scalar, "s_vm", j - 1)
            I(scalar.activation(ssq[:, (j % 2) * NT:(j % 2 + 1) * NT],
                                sxnf[:, (j % 2) * NT:(j % 2 + 1) * NT], AF.Square),
              "s_sq2", 1)

    if l == 0:
        @blk.tensor
        def _(tensor):
            for j in range(NCH):
                W(tensor, "s_sq2", j + 1)
                for hf in range(2):
                    mm = tensor.matmul(psA[0:1, hf * 512:(hf + 1) * 512], sone_f[:],
                                       ssq[:, (j % 2) * NT + hf * 512:(j % 2) * NT + (hf + 1) * 512],
                                       start=(j == 0), stop=(j == NCH - 1))
                I(mm, "s_mb", 1)

    @blk.tensor
    def _(tensor):
        W(tensor, "s_v", cnt["s_v"])
        for bb in range(8):
            if bb >= 1:
                W(tensor, "s_vt", bb)
            I(tensor.transpose(psE[:, 0:128], smx[:, bb * 128:(bb + 1) * 128],
                               siden[:]), "s_mt", 1)

    @blk.vector
    def _(vector):
        for bb in range(8):
            W(vector, "s_mt", bb + 1)
            vector.reduce_max(stok[:, bb:bb + 1], psE[:, 0:128], axis=AX.X)
            vector.drain()
            I(vector.nop(), "s_vt", 1)
        I(vector.nop(), "s_v", 1)

    @blk.tensor
    def _(tensor):
        W(tensor, "s_v", cnt["s_v"])
        for bb in range(8):
            if bb >= 1:
                W(tensor, "s_vt", 8 + bb)
            I(tensor.transpose(psD[0:1, 0:128], stok[:, bb:bb + 1], siden[:]), "s_mt", 1)

    @blk.vector
    def _(vector):
        for bb in range(8):
            W(vector, "s_mt", 8 + bb + 1)
            vector.tensor_copy(row("maxa", bb * 128, (bb + 1) * 128), psD[0:1, 0:128])
            vector.drain()
            I(vector.nop(), "s_vt", 1)
        I(vector.nop(), "s_v", 1)
    MAXA1 = cnt["s_v"]

    @blk.scalar
    def _(scalar):
        W(scalar, "s_v", MAXA1)
        I(scalar.activation(row("maxa"), row("maxa"), AF.Sqrt), "s_sc", 1)
    SQMAXA1 = cnt["s_sc"]

    if l == 0:
        @blk.vector
        def _(vector):
            W(vector, "s_mb", MB0x + NCH)
            W(vector, "s_sc", SQMAXA1)
            I(vector.tensor_copy(row("n2"), psA[0:1, :]), "s_v", 1)
            vector.drain()

        @blk.scalar
        def _(scalar):
            W(scalar, "s_v", cnt["s_v"])
            I(scalar.activation(row("tmp2"), row("n2"), AF.Sqrt), "s_sc", 1)

        @blk.vector
        def _(vector):
            W(vector, "s_sc", cnt["s_sc"])
            vector.tensor_scalar(row("tmp2"), row("tmp2"), 1e-12, 0.0, ALU.max, ALU.add)
            vector.drain()
            vector.reciprocal(row("rms"), row("tmp2"))
            vector.drain()
            vector.tensor_scalar(row("rms"), row("rms"), float(np.sqrt(2048.0)), 0.0,
                                 ALU.mult, ALU.add)
            vector.drain()
            vector.tensor_tensor(row("tmp"), row("rms"), row("maxa"), ALU.mult)
            vector.drain()
            vector.tensor_scalar(row("clp"), row("tmp"), 1e-5, 0.0, ALU.max, ALU.add)
            vector.drain()
            vector.reciprocal(row("tmp2"), row("clp"))
            vector.drain()
            vector.tensor_scalar(row("tmp2"), row("tmp2"), 127.0, 0.0, ALU.mult, ALU.add)
            vector.drain()
            I(vector.tensor_tensor(row("g"), row("rms"), row("tmp2"), ALU.mult),
              "s_v", 1)
            vector.drain()
    else:
        @blk.vector
        def _(vector):
            W(vector, "s_sc", SQMAXA1)
            vector.tensor_scalar(row("d1"), row("maxa"), 1.0 / 127.0, 1e-5,
                                 ALU.mult, ALU.add)
            vector.drain()
            I(vector.reciprocal(row("g"), row("d1")), "s_v", 1)
            vector.drain()

    GROW = cnt["s_v"]

    @blk.tensor
    def _(tensor):
        W(tensor, "s_v", GROW)
        for hf in range(2):
            mm = tensor.matmul(psC[:, hf * 512:(hf + 1) * 512], sone_r[:],
                               row("g", hf * 512, (hf + 1) * 512), start=True, stop=True)
        I(mm, "s_mm", 1)

    @blk.vector
    def _(vector):
        W(vector, "s_mm", 2)
        I(vector.tensor_copy(sbc[:, BO["g"]:BO["g"] + NT], psC[:, :]), "s_v", 1)
        vector.drain()
    GBC = cnt["s_v"]

    @blk.gpsimd
    def _(gpsimd):
        for j in range(NCH):
            ssem = "s_stxa" if (j % 2) == 0 else "s_stxb"
            W(gpsimd, ssem, 32 * (j // 2 + 1))
            if j >= 2:
                W(gpsimd, "s_vx", NCH + j - 1)
            pxr.inc(gpsimd.dma_start(out=sh2[:, (j % 2) * NT:(j % 2 + 1) * NT],
                                     in_=xnf[128 * j:128 * (j + 1), :]), j)

    @blk.vector
    def _(vector):
        W(vector, "s_v", GBC)
        for j in range(NCH):
            pxr.wait(vector, j)
            vector.tensor_tensor(smt[:, :], sh2[:, (j % 2) * NT:(j % 2 + 1) * NT],
                                 sbc[:, BO["g"]:BO["g"] + NT], ALU.mult)
            vector.drain()
            I(vector.tensor_scalar(sxn[:, j * NT:(j + 1) * NT], smt[:, :], RC, -RC,
                                   ALU.add, ALU.add), "s_vx", 1)
            vector.drain()
    TI_DONE = cnt["s_vx"]

    @blk.gpsimd
    def _(gpsimd):
        for n, (m, ri, ci) in enumerate(shard_tiles):
            if n >= 2:
                W(gpsimd, "s_pr", n - 1)
            psh.inc(gpsimd.dma_start(
                out=swsh[:, (n % 2) * 1024:(n % 2 + 1) * 1024],
                in_=shdr[m][128 * ri:128 * (ri + 1), 1024 * ci:1024 * (ci + 1)]), n)

    @blk.vector
    def _(vector):
        for n in range(NSH):
            psh.wait(vector, n)
            I(vector.reduce_sum(scol[:, n:n + 1],
                                swsh[:, (n % 2) * 1024:(n % 2 + 1) * 1024],
                                axis=AX.X, apply_absolute_value=True), "s_pr", 1)
            vector.drain()
        base = 0
        for m in range(NMAT):
            vector.reduce_sum(stok[:, m:m + 1], scol[:, base:base + tiles_per_mat[m]],
                              axis=AX.X)
            vector.drain()
            base += tiles_per_mat[m]
        I(vector.nop(), "s_v", 1)
    FOLD1 = cnt["s_v"]

    @blk.tensor
    def _(tensor):
        W(tensor, "s_v", FOLD1)
        for m in range(NMAT):
            if m >= 1:
                W(tensor, "s_vt", 16 + m)
            I(tensor.transpose(psE[0:1, 0:128], stok[:, m:m + 1], siden[:]), "s_mt", 1)

    @blk.vector
    def _(vector):
        for m in range(NMAT):
            W(vector, "s_mt", 16 + m + 1)
            vector.reduce_sum(srsc[0:1, m:m + 1], psE[0:1, 0:128], axis=AX.X)
            vector.drain()
            I(vector.nop(), "s_vt", 1)
        I(vector.nop(), "s_v", 1)
    PARTS = cnt["s_v"]

    @blk.gpsimd
    def _(gpsimd):
        W(gpsimd, "s_v", PARTS)
        I(gpsimd.dma_start(out=arin[:, :], in_=srsc[0:1, 0:NMAT]), "s_a", 16)
        W(gpsimd, "s_a", 16)
        I(gpsimd.collective_compute(
            "AllReduce", ALU.add, replica_groups=[list(range(NCORES))],
            ins=[arin.ap().opt()], outs=[arout.ap().opt()]), "s_cc", 1)
        W(gpsimd, "s_cc", 1)
        I(gpsimd.dma_start(out=srsc[0:1, 8:8 + NMAT], in_=arout[:, :]), "s_ar", 16)

    @blk.vector
    def _(vector):
        W(vector, "s_ar", 16)
        if l == 0:
            vector.tensor_scalar(srsc[0:1, 16:16 + NMAT], srsc[0:1, 8:8 + NMAT],
                                 1.0 / SZW, 1e-5, ALU.mult, ALU.max)
        else:
            vector.tensor_scalar(srsc[0:1, 16:16 + NMAT], srsc[0:1, 8:8 + NMAT],
                                 1.0 / SZW, 1e-5, ALU.mult, ALU.add)
        vector.drain()
        vector.reciprocal(srsc[0:1, 24:24 + NMAT], srsc[0:1, 16:16 + NMAT])
        vector.drain()
        if l == 0:
            vector.tensor_scalar(row("d1"), row("clp"), srsc[0:1, 16:17], 1.0 / 127.0,
                                 ALU.mult, ALU.mult)
            vector.drain()
        I(vector.nop(), "s_v", 1)
    SCROW = cnt["s_v"]

    @blk.tensor
    def _(tensor):
        W(tensor, "s_v", SCROW)
        nb = 0
        for m in range(NMAT):
            tensor.matmul(psD[:, m:m + 1], sone_r[:], srsc[0:1, 24 + m:25 + m],
                          start=(nb == 0), stop=False)
            nb += 1
            mm = tensor.matmul(psD[:, 8 + m:9 + m], sone_r[:], srsc[0:1, 16 + m:17 + m],
                               start=False, stop=(m == NMAT - 1))
            nb += 1
        I(mm, "s_mm", 1)

    @blk.vector
    def _(vector):
        W(vector, "s_mm", 3)
        vector.tensor_copy(swsc[:, 0:NMAT], psD[:, 0:NMAT])
        vector.drain()
        I(vector.tensor_copy(swsc[:, NMAT:2 * NMAT], psD[:, 8:8 + NMAT]), "s_v", 1)
        vector.drain()
    SCBC = cnt["s_v"]

    @blk.tensor
    def _(tensor):
        W(tensor, "s_v", SCBC)
        for hf in range(2):
            mm = tensor.matmul(psA[:, hf * 512:(hf + 1) * 512], sone_r[:],
                               row("d1", hf * 512, (hf + 1) * 512), start=True, stop=True)
        I(mm, "s_mm", 1)

    @blk.vector
    def _(vector):
        W(vector, "s_mm", 4)
        I(vector.tensor_copy(sbc[:, BO["d"]:BO["d"] + NT], psA[:, :]), "s_v", 1)
        vector.drain()
    DBC = cnt["s_v"]

    NB1 = 16 * 16 if l == 0 else 16 * 32
    EV0 = cnt["s_ev"]
    GL0 = cnt["s_gl"]

    def emit_quant(vector, n):
        m = wtiles[n][0]
        pwl.wait(vector, n)
        if n >= 2:
            W(vector, "s_up", n - 1)
        sl = (n % 2) * 512
        I(vector.tensor_scalar(swqs[:, :], swst[:, sl:sl + 512],
                               swsc[:, m:m + 1], -1.0, ALU.mult, ALU.max), "s_wq", 1)
        vector.drain()
        vector.tensor_scalar(swqs[:, :], swqs[:, :], 1.0, RC, ALU.min, ALU.add)
        vector.drain()
        I(vector.tensor_scalar(swq[:, sl:sl + 512], swqs[:, :], -RC, 0.0,
                               ALU.add, ALU.add), "s_wu", 1)
        vector.drain()

    if l == 0:
        def _stats_piece(vector, p):
            hb, b8 = p // 8, p % 8
            t4, hf = b8 % 4, b8 // 4
            W(vector, "s_hsq", p + 1)
            q4 = (p % 8) * 512
            if hb == 0 and t4 == 0:
                I(vector.tensor_copy(smt[:, hf * 512:(hf + 1) * 512],
                                     ssq[:, q4:q4 + 512]), "s_mh", 1)
            else:
                I(vector.tensor_tensor(smt[:, hf * 512:(hf + 1) * 512],
                                       smt[:, hf * 512:(hf + 1) * 512],
                                       ssq[:, q4:q4 + 512], ALU.max), "s_mh", 1)
            vector.drain()
            if hb == 0 and t4 == 0:
                I(vector.tensor_copy(sacc[:, hf * 512:(hf + 1) * 512],
                                     ssq[:, q4:q4 + 512]), "s_va", 1)
            else:
                I(vector.tensor_tensor(sacc[:, hf * 512:(hf + 1) * 512],
                                       sacc[:, hf * 512:(hf + 1) * 512],
                                       ssq[:, q4:q4 + 512], ALU.add), "s_va", 1)
            vector.drain()

        def emit_evac(vector, hb):
            W(vector, "s_up", (hb + 1) * 16)
            for b8 in range(8):
                t4, hf = b8 % 4, b8 // 4
                if hb >= 1:
                    W(vector, "s_gl", GL0 + (hb - 1) * 8 + b8 + 1)
                I(vector.tensor_tensor(sfo[:, b8 * 512:(b8 + 1) * 512], PB8[b8],
                                       sbc[:, BO["d"] + hf * 512:BO["d"] + (hf + 1) * 512],
                                       ALU.mult), "s_ev", 1)
                vector.drain()
            if hb >= 1:
                for b8 in range(8):
                    _stats_piece(vector, (hb - 1) * 8 + b8)

        @blk.vector
        def _(vector):
            for hb in range(16):
                if hb >= 1:
                    emit_evac(vector, hb - 1)
                for n in range(hb * 16, hb * 16 + 16):
                    emit_quant(vector, n)
            emit_evac(vector, 15)
            for b8 in range(8):
                _stats_piece(vector, 15 * 8 + b8)

        @blk.scalar
        def _(scalar):
            for hb in range(16):
                for b8 in range(8):
                    p = hb * 8 + b8
                    t4, hf = b8 % 4, b8 // 4
                    W(scalar, "s_ev", EV0 + p + 1)
                    scalar.activation(sfo[:, b8 * 512:(b8 + 1) * 512],
                                      sfo[:, b8 * 512:(b8 + 1) * 512], AF.Gelu)
                    scalar.drain()
                    sl8 = t4 * 2 + hf
                    if hb >= 1:
                        W(scalar, "s_sth", 16 * hb)
                    I(scalar.activation(shs[:, sl8 * 512:(sl8 + 1) * 512],
                                        sfo[:, b8 * 512:(b8 + 1) * 512], AF.Gelu),
                      "s_gl", 1)
                    scalar.drain()
                    if p >= 8:
                        W(scalar, "s_va", p - 7)
                        W(scalar, "s_mh", p - 7)
                    I(scalar.activation(ssq[:, (p % 8) * 512:(p % 8) * 512 + 512],
                                        shs[:, sl8 * 512:(sl8 + 1) * 512], AF.Square),
                      "s_hsq", 1)
                    scalar.drain()
    else:
        def _tree_piece(vector, p):
            hb, b8 = p // 8, p % 8
            t4, hf = b8 % 4, b8 // 4
            W(vector, "s_hsq", p + 1)
            q4 = (p % 8) * 512
            if hb == 0 and t4 == 0:
                I(vector.tensor_copy(smt[:, hf * 512:(hf + 1) * 512],
                                     ssq[:, q4:q4 + 512]), "s_mh", 1)
            else:
                I(vector.tensor_tensor(smt[:, hf * 512:(hf + 1) * 512],
                                       smt[:, hf * 512:(hf + 1) * 512],
                                       ssq[:, q4:q4 + 512], ALU.max), "s_mh", 1)
            vector.drain()

        def emit_evac_gate(vector, hb):
            W(vector, "s_up", hb * 32 + 16)
            for b8 in range(8):
                if hb >= 1:
                    W(vector, "s_gl", GL0 + (hb - 1) * 8 + b8 + 1)
                I(vector.scalar_tensor_tensor(
                    sfo[:, b8 * 512:(b8 + 1) * 512], PB8[b8],
                    swsc[:, NMAT + 0:NMAT + 1],
                    sbc[:, BO["d"] + (b8 // 4) * 512:BO["d"] + (b8 // 4 + 1) * 512],
                    ALU.mult, ALU.mult), "s_ev", 1)
                vector.drain()

        def emit_evac_up(vector, hb):
            W(vector, "s_up", hb * 32 + 32)
            for b8 in range(8):
                p = hb * 8 + b8
                t4, hf = b8 % 4, b8 // 4
                W(vector, "s_gl", GL0 + hb * 8 + b8 + 1)
                vector.scalar_tensor_tensor(
                    swqs[:, :], PB8[b8], swsc[:, NMAT + 1:NMAT + 2],
                    sbc[:, BO["d"] + hf * 512:BO["d"] + (hf + 1) * 512],
                    ALU.mult, ALU.mult)
                vector.drain()
                I(vector.nop(), "s_ev", 1)
                if hb >= 1:
                    W(vector, "s_sth", 16 * hb)
                if p >= 8:
                    W(vector, "s_hsq", p - 7)
                sl8 = t4 * 2 + hf
                I(vector.tensor_tensor(shs[:, sl8 * 512:(sl8 + 1) * 512],
                                       swqs[:, :], sgt[:, b8 * 512:(b8 + 1) * 512],
                                       ALU.mult), "s_pc", 1)
                vector.drain()
                if p >= 1:
                    _tree_piece(vector, p - 1)

        @blk.vector
        def _(vector):
            for hb in range(16):
                if hb >= 1:
                    emit_evac_up(vector, hb - 1)
                for n in range(hb * 32, hb * 32 + 16):
                    emit_quant(vector, n)
                emit_evac_gate(vector, hb)
                for n in range(hb * 32 + 16, hb * 32 + 32):
                    emit_quant(vector, n)
            emit_evac_up(vector, 15)
            _tree_piece(vector, 127)

        @blk.scalar
        def _(scalar):
            for hb in range(16):
                for b8 in range(8):
                    p = hb * 8 + b8
                    W(scalar, "s_ev", EV0 + hb * 16 + b8 + 1)
                    if hb >= 1:
                        W(scalar, "s_pc", (hb - 1) * 8 + b8 + 1)
                    I(scalar.activation(sgt[:, b8 * 512:(b8 + 1) * 512],
                                        sfo[:, b8 * 512:(b8 + 1) * 512], AF.Silu),
                      "s_gl", 1)
                    scalar.drain()
                    W(scalar, "s_pc", p + 1)
                    if p >= 8:
                        W(scalar, "s_mh", p - 7)
                    sl8 = (b8 % 4) * 2 + b8 // 4
                    I(scalar.activation(ssq[:, (p % 8) * 512:(p % 8) * 512 + 512],
                                        shs[:, sl8 * 512:(sl8 + 1) * 512], AF.Square),
                      "s_hsq", 1)
                    scalar.drain()

    @blk.tensor
    def _(tensor):
        W(tensor, "s_vx", TI_DONE)
        if l == 0:
            for hb in range(16):
                for ic in range(16):
                    n = hb * 16 + ic
                    W(tensor, "s_wu", n + 1)
                    sl = (n % 2) * 512
                    for b8 in range(8):
                        t4, hf = b8 % 4, b8 // 4
                        if ic == 0 and hb >= 1:
                            W(tensor, "s_ev", EV0 + (hb - 1) * 8 + b8 + 1)
                        mm = tensor.matmul(
                            PB8[b8], swq[:, sl + t4 * 128:sl + (t4 + 1) * 128],
                            sxn[:, ic * NT + hf * 512:ic * NT + (hf + 1) * 512],
                            start=(ic == 0), stop=(ic == 15))
                    I(mm, "s_up", 1)
        else:
            for hb in range(16):
                for g_or_u in range(2):
                    for ic in range(16):
                        n = hb * 32 + g_or_u * 16 + ic
                        W(tensor, "s_wu", n + 1)
                        sl = (n % 2) * 512
                        for b8 in range(8):
                            t4, hf = b8 % 4, b8 // 4
                            if ic == 0:
                                prev = hb * 16 if g_or_u == 1 else (hb - 1) * 16 + 8
                                if hb >= 1 or g_or_u == 1:
                                    W(tensor, "s_ev", EV0 + prev + b8 + 1)
                            mm = tensor.matmul(
                                PB8[b8], swq[:, sl + t4 * 128:sl + (t4 + 1) * 128],
                                sxn[:, ic * NT + hf * 512:ic * NT + (hf + 1) * 512],
                                start=(ic == 0), stop=(ic == 15))
                        I(mm, "s_up", 1)

    PC_SEM = "s_gl" if l == 0 else "s_pc"
    PC0 = GL0 if l == 0 else 0

    @blk.gpsimd
    def _(gpsimd):
        for hb in range(16):
            W(gpsimd, PC_SEM, PC0 + (hb + 1) * 8)
            dst = hscr.ap().rearrange("(t p) e -> p t e", p=128)[:, 4 * hb:4 * hb + 4, :]
            srcap = shs.ap().rearrange("p (t e) -> p t e", t=4)
            I(gpsimd.dma_start(out=dst, in_=srcap), "s_sth", 16)

    MT0 = cnt["s_mt"]
    VT0 = cnt["s_vt"]
    TREE_DONE = cnt["s_mh"]

    if l == 0:
        @blk.tensor
        def _(tensor):
            W(tensor, "s_va", cnt["s_va"])
            W(tensor, "s_ev", cnt["s_ev"])
            for hf in range(2):
                mm = tensor.matmul(psA[0:1, hf * 512:(hf + 1) * 512], sone_f[:],
                                   sacc[:, hf * 512:(hf + 1) * 512],
                                   start=True, stop=True)
            I(mm, "s_mb", 1)
        MB1 = cnt["s_mb"]

    @blk.tensor
    def _(tensor):
        W(tensor, "s_mh", TREE_DONE)
        for bb in range(8):
            if bb >= 1:
                W(tensor, "s_vt", VT0 + bb)
            I(tensor.transpose(psE[:, 0:128], smt[:, bb * 128:(bb + 1) * 128],
                               siden[:]), "s_mt", 1)

    @blk.vector
    def _(vector):
        for bb in range(8):
            W(vector, "s_mt", MT0 + bb + 1)
            vector.reduce_max(stok[:, bb:bb + 1], psE[:, 0:128], axis=AX.X)
            vector.drain()
            I(vector.nop(), "s_vt", 1)
        I(vector.nop(), "s_v", 1)

    @blk.tensor
    def _(tensor):
        W(tensor, "s_v", cnt["s_v"])
        for bb in range(8):
            if bb >= 1:
                W(tensor, "s_vt", VT0 + 8 + bb)
            I(tensor.transpose(psD[0:1, 0:128], stok[:, bb:bb + 1], siden[:]), "s_mt", 1)

    @blk.vector
    def _(vector):
        for bb in range(8):
            W(vector, "s_mt", MT0 + 8 + bb + 1)
            vector.tensor_copy(row("maxa", bb * 128, (bb + 1) * 128), psD[0:1, 0:128])
            vector.drain()
            I(vector.nop(), "s_vt", 1)
        I(vector.nop(), "s_v", 1)
    MAXH = cnt["s_v"]

    @blk.scalar
    def _(scalar):
        W(scalar, "s_v", MAXH)
        I(scalar.activation(row("maxa"), row("maxa"), AF.Sqrt), "s_sc", 1)
    SQMAXH = cnt["s_sc"]

    if l == 0:
        @blk.vector
        def _(vector):
            W(vector, "s_mb", MB1)
            I(vector.tensor_copy(row("n2"), psA[0:1, :]), "s_v", 1)
            vector.drain()

        @blk.scalar
        def _(scalar):
            W(scalar, "s_v", cnt["s_v"])
            I(scalar.activation(row("tmp2"), row("n2"), AF.Sqrt), "s_sc", 1)
        SC2 = cnt["s_sc"]

        @blk.vector
        def _(vector):
            W(vector, "s_sc", SC2)
            W(vector, "s_v", MAXH)
            W(vector, "s_sc", SC2)
            vector.tensor_scalar(row("tmp2"), row("tmp2"), 1e-12, 0.0, ALU.max, ALU.add)
            vector.drain()
            vector.reciprocal(row("rms"), row("tmp2"))
            vector.drain()
            vector.tensor_scalar(row("rms"), row("rms"), float(np.sqrt(8192.0)), 0.0,
                                 ALU.mult, ALU.add)
            vector.drain()
            vector.tensor_tensor(row("tmp"), row("rms"), row("maxa"), ALU.mult)
            vector.drain()
            vector.tensor_scalar(row("clp"), row("tmp"), 1e-5, 0.0, ALU.max, ALU.add)
            vector.drain()
            vector.reciprocal(row("tmp2"), row("clp"))
            vector.drain()
            vector.tensor_scalar(row("tmp2"), row("tmp2"), 127.0, 0.0, ALU.mult, ALU.add)
            vector.drain()
            vector.tensor_tensor(row("g"), row("rms"), row("tmp2"), ALU.mult)
            vector.drain()
            I(vector.tensor_scalar(row("d1"), row("clp"), srsc[0:1, 17:18], 1.0 / 127.0,
                                   ALU.mult, ALU.mult), "s_v", 1)
            vector.drain()
    else:
        @blk.vector
        def _(vector):
            W(vector, "s_v", MAXH)
            W(vector, "s_sc", SQMAXH)
            vector.tensor_scalar(row("tmp"), row("maxa"), 1.0 / 127.0, 1e-5,
                                 ALU.mult, ALU.add)
            vector.drain()
            vector.reciprocal(row("g"), row("tmp"))
            vector.drain()
            I(vector.tensor_scalar(row("d1"), row("tmp"), srsc[0:1, 18:19], 0.0,
                                   ALU.mult, ALU.add), "s_v", 1)
            vector.drain()
    G2ROW = cnt["s_v"]

    @blk.tensor
    def _(tensor):
        W(tensor, "s_v", G2ROW)
        for hf in range(2):
            tensor.matmul(psC[:, hf * 512:(hf + 1) * 512], sone_r[:],
                          row("g", hf * 512, (hf + 1) * 512), start=True, stop=True)
        for hf in range(2):
            mm = tensor.matmul(psB[:, hf * 512:(hf + 1) * 512], sone_r[:],
                               row("d1", hf * 512, (hf + 1) * 512), start=True, stop=True)
        I(mm, "s_mm", 1)

    @blk.vector
    def _(vector):
        W(vector, "s_mm", cnt["s_mm"])
        vector.tensor_copy(sbc[:, BO["g"]:BO["g"] + NT], psC[:, :])
        vector.drain()
        I(vector.tensor_copy(sbc[:, BO["d"]:BO["d"] + NT], psB[:, :]), "s_v", 1)
        vector.drain()
    G2BC = cnt["s_v"]

    VX2_pre = cnt["s_vx"]
    U0 = cnt["s_up"]
    NOFF = NB1
    EV2 = cnt["s_ev"]

    @blk.gpsimd
    def _(gpsimd):
        for ic in range(HCH + 2):
            if ic < HCH:
                if ic >= 2:
                    W(gpsimd, "s_vx", VX2_pre + ic - 1)
                W(gpsimd, "s_sth", 16 * (ic // 4 + 1))
                phr.inc(gpsimd.dma_start(out=sh2[:, (ic % 2) * NT:(ic % 2 + 1) * NT],
                                         in_=hscr[128 * ic:128 * (ic + 1), :]), ic)
            if ic >= 2:
                jc = ic - 2
                W(gpsimd, "s_vx", VX2_pre + jc + 1)
                ptw.inc(gpsimd.dma_start(out=t2scr[128 * jc:128 * (jc + 1), :],
                                         in_=sti2[:, (jc % 2) * NT:(jc % 2 + 1) * NT]), jc)
        for ob in range(1, 4):
            for ic in range(HCH):
                m = (ob - 1) * HCH + ic
                ptw.wait(gpsimd, ic)
                if m >= 2:
                    W(gpsimd, "s_up", U0 + HCH + m - 1)
                phr2.inc(gpsimd.dma_start(out=st2r[:, (m % 2) * NT:(m % 2 + 1) * NT],
                                          in_=t2scr[128 * ic:128 * (ic + 1), :]), m)

    def emit_round2(vector, ic):
        phr.wait(vector, ic)
        vector.tensor_tensor(smt[:, :], sh2[:, (ic % 2) * NT:(ic % 2 + 1) * NT],
                             sbc[:, BO["g"]:BO["g"] + NT], ALU.mult)
        vector.drain()
        if ic >= 2:
            W(vector, "s_up", U0 + ic - 1)
            ptw.wait(vector, ic - 2)
        I(vector.tensor_scalar(sti2[:, (ic % 2) * NT:(ic % 2 + 1) * NT],
                               smt[:, :], RC, -RC, ALU.add, ALU.add), "s_vx", 1)
        vector.drain()

    def emit_evac2(vector, ob):
        W(vector, "s_up", U0 + (ob + 1) * HCH)
        for t4 in range(4):
            ot = ob * 4 + t4
            if ob >= 1:
                pstf.wait(vector, (ob - 1) * 4 + t4)
            for hf in range(2):
                b8 = hf * 4 + t4
                I(vector.tensor_tensor(
                    sfo[:, t4 * 1024 + hf * 512:t4 * 1024 + (hf + 1) * 512],
                    PB8[b8],
                    sbc[:, BO["d"] + hf * 512:BO["d"] + (hf + 1) * 512], ALU.mult),
                  "s_ev", 1)
                vector.drain()
            I(vector.nop(), "s_vf", 1)

    @blk.vector
    def _(vector):
        W(vector, "s_v", G2BC)
        for ob in range(4):
            if ob >= 1:
                emit_evac2(vector, ob - 1)
            for ic in range(HCH):
                emit_quant(vector, NOFF + ob * HCH + ic)
                if ob == 0:
                    emit_round2(vector, ic)
        emit_evac2(vector, 3)

    @blk.tensor
    def _(tensor):
        for ob in range(4):
            for ic in range(HCH):
                n = NOFF + ob * HCH + ic
                W(tensor, "s_wu", n + 1)
                if ob == 0:
                    W(tensor, "s_vx", VX2_pre + ic + 1)
                    rhs_base = (ic % 2) * NT
                    rhs_t = sti2
                else:
                    m = (ob - 1) * HCH + ic
                    phr2.wait(tensor, m)
                    rhs_base = (m % 2) * NT
                    rhs_t = st2r
                sl = (n % 2) * 512
                for b8 in range(8):
                    t4, hf = b8 % 4, b8 // 4
                    if ic == 0 and ob >= 1:
                        W(tensor, "s_ev", EV2 + (ob - 1) * 8 + (b8 % 4) * 2 + b8 // 4 + 1)
                    mm = tensor.matmul(
                        PB8[b8], swq[:, sl + t4 * 128:sl + (t4 + 1) * 128],
                        rhs_t[:, rhs_base + hf * 512:rhs_base + (hf + 1) * 512],
                        start=(ic == 0), stop=(ic == HCH - 1))
                I(mm, "s_up", 1)

    @blk.scalar
    def _(scalar):
        for ot in range(16):
            W(scalar, "s_vf", ot + 1)
            pstf.inc(scalar.dma_start(out=ffn_o[128 * ot:128 * (ot + 1), :],
                                      in_=sfo[:, (ot % 4) * 1024:(ot % 4) * 1024 + NT]), ot)
        for t4 in range(4):
            pstf.wait(scalar, 12 + t4)

    @blk.sync
    def _(sync):
        W(sync, "s_stxa")
        W(sync, "s_stxb")

    c.close()
    return c


def build_B(l, debug=False):
    c = Ctx(f"B{l}", debug=debug)
    nc = c.nc
    SEQ = 4096
    SC = 1.0 / np.sqrt(128.0) if l == 0 else 1.0 / 64.0

    xnfull = din(nc, "xnfull", [D, SEQ], BF16)
    if l == 0:
        wsl = din(nc, "wqkv_sl", [D, 1536])
        bsl = din(nc, "bqkv_sl", [1536, 1])
        wo = din(nc, "wproj_sl", [512, D])
    else:
        wq_ = din(nc, "wq_sl", [D, 512])
        wk_ = din(nc, "wk_sl", [D, 512])
        wv_ = din(nc, "wv_sl", [D, 512])
        wo = din(nc, "wo_sl", [512, D])
        bands = din(nc, "bands", [128, 4 * 512], BF16)
    ones_f = din(nc, "ones_f", [128, 1])
    ones_r = din(nc, "ones_r", [1, 128])
    ones_fb = din(nc, "ones_fb", [128, 1], BF16)
    ones_rb = din(nc, "ones_rb", [1, 128], BF16)
    iden_b = din(nc, "iden_b", [128, 128], BF16)
    iden_f = din(nc, "iden_f", [128, 128])
    part = dout(nc, "part", [D, SEQ])

    sxn = c.sb("sxn", [128, 16 * 512], BF16)
    swb = c.sb("swb", [128, 16 * 1536], BF16)
    swo = c.sb("swo", [128, 4 * 2048], BF16)
    sq = c.sb("sq", [128, 2 * SEQ], BF16)
    sk = c.sb("sk", [128, 2 * SEQ], BF16)
    sv = c.sb("sv", [128, 2 * 32 * 130], BF16)
    sexp = c.sb("sexp", [128, 2 * 512], BF16)
    sctx = c.sb("sctx", [128, 32 * 128], BF16)
    sctxT = c.sb("sctxT", [128, 4 * SEQ], BF16)
    sfoB = c.sb("sfoB", [128, 8 * 512])
    ssc = sfoB
    sbnd = c.sb("sbnd", [128, 4 * 512], BF16) if l == 1 else None
    srw = c.sb("srw", [1, 64])
    sbias = c.sb("sbias", [128, 2])
    szq = c.sb("szq", [128, 2])
    sbq = c.sb("sbq", [128, 12]) if l == 0 else None
    sbvr = c.sb("sbvr", [1, 512], BF16) if l == 0 else None
    ssuf = c.sb("ssuf", [1, 9 * 129]) if l == 1 else None
    ssufb = c.sb("ssufb", [1, 8 * 129], BF16) if l == 1 else None
    sone_f = c.sb("sone_f", [128, 1])
    sone_r = c.sb("sone_r", [1, 128])
    sone_fb = c.sb("sone_fb", [128, 1], BF16)
    sone_rb = c.sb("sone_rb", [1, 128], BF16)
    siden = c.sb("siden", [128, 128], BF16)
    siden_f = c.sb("siden_f", [128, 128])

    PB = [c.ps(f"pb{i}", [128, 512]) for i in range(8)]

    sems, cnt = {}, {}
    for sn in ["s_k", "s_w", "s_mq", "s_vq", "s_mk", "s_vk", "s_ek", "s_ak",
               "s_cv", "s_mb", "s_vb", "s_sb", "s_mt", "s_vt", "s_mp", "s_vp",
               "s_sf", "s_bs"]:
        sems[sn] = c.sem(sn)
        cnt[sn] = 0
    pxn = PDma(c, "pxn", 1)
    pst = PDma(c, "pst", 8)

    blk = c.block()

    def I(instr, s, n=1):
        cnt[s] += n
        instr.then_inc(sems[s], n)
        return cnt[s]

    def W(eng, s, v=None):
        eng.wait_ge(sems[s], cnt[s] if v is None else v)

    @blk.gpsimd
    def _(gpsimd):
        if l == 0:
            for j in range(NCH):
                I(gpsimd.dma_start(out=swb[:, j * 1536:(j + 1) * 1536],
                                   in_=wsl[128 * j:128 * (j + 1), :]), "s_w", 16)
        else:
            for j in range(NCH):
                I(gpsimd.dma_start(out=swb[:, j * 1536:j * 1536 + 512],
                                   in_=wq_[128 * j:128 * (j + 1), :]), "s_w", 16)
                I(gpsimd.dma_start(out=swb[:, j * 1536 + 512:j * 1536 + 1024],
                                   in_=wk_[128 * j:128 * (j + 1), :]), "s_w", 16)
                I(gpsimd.dma_start(out=swb[:, j * 1536 + 1024:j * 1536 + 1536],
                                   in_=wv_[128 * j:128 * (j + 1), :]), "s_w", 16)
        for j in range(4):
            I(gpsimd.dma_start(out=swo[:, j * 2048:(j + 1) * 2048],
                               in_=wo[128 * j:128 * (j + 1), :]), "s_w", 16)
        if l == 0:
            for t in range(12):
                I(gpsimd.dma_start(out=sbq[:, t:t + 1],
                                   in_=bsl[128 * t:128 * (t + 1), :]), "s_w", 16)
        gpsimd.memset(sv.ap().rearrange("p (t w) -> p t w", w=130)[:, :, 128:129], 1.0)
        gpsimd.drain()
        I(gpsimd.nop(), "s_bs", 1)

    @blk.sync
    def _(sync):
        I(sync.dma_start(out=sone_f[:], in_=ones_f[:, :]), "s_k", 16)
        I(sync.dma_start(out=sone_r[:], in_=ones_r[:, :]), "s_k", 16)
        I(sync.dma_start(out=sone_fb[:], in_=ones_fb[:, :]), "s_k", 16)
        I(sync.dma_start(out=sone_rb[:], in_=ones_rb[:, :]), "s_k", 16)
        I(sync.dma_start(out=siden[:], in_=iden_b[:, :]), "s_k", 16)
        I(sync.dma_start(out=siden_f[:], in_=iden_f[:, :]), "s_k", 16)
        if l == 1:
            I(sync.dma_start(out=sbnd[:, :], in_=bands[:, :]), "s_k", 16)
        K_ALL = cnt["s_k"]
        for hp in range(2):
            for tb in range(8):
                i = hp * 8 + tb
                if i >= 1:
                    W(sync, "s_mq", i * 8)
                src = xnfull.ap().rearrange("(j p) t -> p j t", p=128)[:, :, 512 * tb:512 * (tb + 1)]
                pxn.inc(sync.dma_start(out=sxn.ap().rearrange("p (j t) -> p j t", t=512),
                                       in_=src), i)

    def qkv_phase(hp):
        @blk.tensor
        def _(tensor):
            if hp == 0:
                W(tensor, "s_k", 96 if l == 0 else 112)
                W(tensor, "s_w", cnt["s_w"])
                W(tensor, "s_bs", 1)
                if l == 0:
                    W(tensor, "s_vb", cnt["s_vb"])
            else:
                W(tensor, "s_vt", cnt["s_vt"])
                W(tensor, "s_cv", cnt["s_cv"])
                for bk in range(8):
                    head_state["ctxT_free"][bk % 4] = None
                    head_state["av_free"][bk % 4] = None
            for tb in range(8):
                i = hp * 8 + tb
                pxn.wait(tensor, i)
                sb = 0
                if i >= 1:
                    W(tensor, "s_vq", i * 8)
                for h2 in range(2):
                    hh = hp * 2 + h2
                    for j in range(NCH):
                        mm = tensor.matmul(PB[h2][:, :],
                                           swb[:, j * 1536 + hh * 128:j * 1536 + (hh + 1) * 128],
                                           sxn[:, sb + j * 512:sb + (j + 1) * 512],
                                           start=(j == 0), stop=(j == NCH - 1))
                    I(mm, "s_mq", 1)
                    for j in range(NCH):
                        mm = tensor.matmul(PB[2 + h2][:, :],
                                           swb[:, j * 1536 + 512 + hh * 128:j * 1536 + 512 + (hh + 1) * 128],
                                           sxn[:, sb + j * 512:sb + (j + 1) * 512],
                                           start=(j == 0), stop=(j == NCH - 1))
                    I(mm, "s_mq", 1)
                for tt in range(4):
                    for j in range(NCH):
                        last = (j == NCH - 1) and (l == 1)
                        mm = tensor.matmul(PB[4 + tt][:, 0:256],
                                           sxn[:, sb + j * 512 + tt * 128:sb + j * 512 + (tt + 1) * 128],
                                           swb[:, j * 1536 + 1024 + hp * 256:j * 1536 + 1024 + (hp + 1) * 256],
                                           start=(j == 0), stop=last)
                    if l == 0:
                        mm = tensor.matmul(PB[4 + tt][:, 0:256], sone_rb[:],
                                           sbvr[0:1, hp * 256:(hp + 1) * 256],
                                           start=False, stop=True)
                    I(mm, "s_mq", 1)

        @blk.vector
        def _(vector):
            for tb in range(8):
                i = hp * 8 + tb
                base = i * 8
                for h2 in range(2):
                    W(vector, "s_mq", base + 2 * h2 + 1)
                    if l == 0:
                        vector.tensor_scalar(sq[:, h2 * SEQ + tb * 512:h2 * SEQ + (tb + 1) * 512],
                                             PB[h2][:, :], sbq[:, (hp * 2 + h2):(hp * 2 + h2) + 1],
                                             0.0, ALU.add, ALU.add)
                    else:
                        vector.tensor_copy(sq[:, h2 * SEQ + tb * 512:h2 * SEQ + (tb + 1) * 512],
                                           PB[h2][:, :])
                    vector.drain()
                    I(vector.nop(), "s_vq", 1)
                    W(vector, "s_mq", base + 2 * h2 + 2)
                    if l == 0:
                        vector.tensor_scalar(sk[:, h2 * SEQ + tb * 512:h2 * SEQ + (tb + 1) * 512],
                                             PB[2 + h2][:, :], sbq[:, 4 + hp * 2 + h2:5 + hp * 2 + h2],
                                             0.0, ALU.add, ALU.add)
                    else:
                        vector.tensor_copy(sk[:, h2 * SEQ + tb * 512:h2 * SEQ + (tb + 1) * 512],
                                           PB[2 + h2][:, :])
                    vector.drain()
                    I(vector.nop(), "s_vq", 1)
                for tt in range(4):
                    W(vector, "s_mq", base + 5 + tt)
                    for h2 in range(2):
                        vector.tensor_copy(
                            sv.ap().rearrange("p (s t w) -> p s t w", s=2, w=130)[:, h2, tb * 4 + tt, 0:128],
                            PB[4 + tt][:, h2 * 128:(h2 + 1) * 128])
                        vector.drain()
                    I(vector.nop(), "s_vq", 1)

    if l == 0:
        @blk.vector
        def _(vector):
            W(vector, "s_w", cnt["s_w"])
            I(vector.nop(), "s_vb", 1)

        @blk.tensor
        def _(tensor):
            W(tensor, "s_vb", 1)
            W(tensor, "s_k", 96)
            for t in range(4):
                mm = tensor.matmul(PB[t][0:1, 0:128],
                                   sbq[:, 8 + t:9 + t], siden_f[:],
                                   start=True, stop=True)
                I(mm, "s_mb", 1)

        @blk.vector
        def _(vector):
            W(vector, "s_mb", 4)
            for t in range(4):
                vector.tensor_copy(sbvr[0:1, t * 128:(t + 1) * 128], PB[t][0:1, 0:128])
                vector.drain()
            I(vector.nop(), "s_vb", 1)

    def bounds_head(hp, h2):
        hh = hp * 2 + h2
        MB_b = cnt["s_mb"]
        VB0 = cnt["s_vb"]

        @blk.scalar
        def _(scalar):
            W(scalar, "s_vq", 64 * (hp + 1))
            for side in range(2):
                for pc in range(8):
                    i = side * 8 + pc
                    W(scalar, "s_mb", MB_b + i)
                    srcbuf = sq if side == 0 else sk
                    I(scalar.activation(ssc[:, 3584:4096], srcbuf[:, h2 * SEQ + pc * 512:h2 * SEQ + (pc + 1) * 512],
                                        AF.Square), "s_sb", 1)

        SB0 = cnt["s_sb"] - 16
        MB0 = cnt["s_mb"]

        @blk.tensor
        def _(tensor):
            for bk in range(2):
                if head_state["ctxT_free"][bk] is not None:
                    W(tensor, "s_vt", head_state["ctxT_free"][bk])
                    head_state["ctxT_free"][bk] = None
            for i in range(16):
                W(tensor, "s_sb", SB0 + i + 1)
                if i >= 1:
                    W(tensor, "s_vb", VB0 + i)
                I(tensor.matmul(PB[0][0:1, 0:512], sone_f[:], ssc[:, 3584:4096],
                                start=True, stop=True), "s_mb", 1)

        @blk.vector
        def _(vector):
            for i in range(16):
                W(vector, "s_mb", MB0 + i + 1)
                vector.reduce_max(srw[0:1, 8 + i % 2:9 + i % 2] if False else srw[0:1, 8:9],
                                  PB[0][0:1, 0:512], axis=AX.X)
                vector.drain()
                side = i // 8
                if i % 8 == 0:
                    vector.tensor_copy(srw[0:1, side:side + 1], srw[0:1, 8:9])
                else:
                    vector.tensor_tensor(srw[0:1, side:side + 1], srw[0:1, side:side + 1],
                                         srw[0:1, 8:9], ALU.max)
                vector.drain()
                I(vector.nop(), "s_vb", 1)
            I(vector.tensor_tensor(srw[0:1, 2:3], srw[0:1, 0:1], srw[0:1, 1:2],
                                   ALU.mult), "s_vb", 1)
            vector.drain()

        @blk.scalar
        def _(scalar):
            W(scalar, "s_vb", cnt["s_vb"])
            I(scalar.activation(srw[0:1, 2:3], srw[0:1, 2:3], AF.Sqrt), "s_sb", 1)
            scalar.drain()
            if l == 1:
                I(scalar.activation(srw[0:1, 4:5], srw[0:1, 2:3], AF.Exp,
                                    scale=-SC), "s_sb", 1)
                scalar.drain()

        @blk.vector
        def _(vector):
            W(vector, "s_sb", cnt["s_sb"])
            I(vector.tensor_scalar(srw[0:1, 3:4], srw[0:1, 2:3], -SC, 0.0,
                                   ALU.mult, ALU.add), "s_vb", 1)
            vector.drain()

        @blk.tensor
        def _(tensor):
            W(tensor, "s_vb", cnt["s_vb"])
            I(tensor.matmul(PB[2 + h2][:, 0:1], sone_r[:], srw[0:1, 3:4],
                            start=True, stop=True), "s_mb", 1)

        @blk.vector
        def _(vector):
            W(vector, "s_mb", cnt["s_mb"])
            I(vector.tensor_copy(sbias[:, h2:h2 + 1], PB[2 + h2][:, 0:1]), "s_vb", 1)
            vector.drain()

    def suffix_head(hp, h2):
        VB_b = cnt["s_vb"]

        @blk.tensor
        def _(tensor):
            for b in range(8):
                if b >= 1:
                    W(tensor, "s_vb", VB_b + b)
                for t in range(4):
                    mm = tensor.matmul(
                        PB[1][0:1, 0:129], sone_fb[:],
                        sv.ap().rearrange("p (s t w) -> p s t w", s=2, w=130)[:, h2, b * 4 + t, 0:129],
                        start=(t == 0), stop=(t == 3))
                I(mm, "s_mb", 1)

        MB0 = cnt["s_mb"] - 8

        @blk.vector
        def _(vector):
            for b in range(8):
                W(vector, "s_mb", MB0 + b + 1)
                vector.tensor_copy(ssuf[0:1, b * 129:(b + 1) * 129], PB[1][0:1, 0:129])
                vector.drain()
                I(vector.nop(), "s_vb", 1)
            vector.tensor_copy(ssuf[0:1, 8 * 129:9 * 129], ssuf[0:1, 7 * 129:8 * 129])
            vector.drain()
            I(vector.tensor_scalar(ssufb[0:1, 6 * 129:7 * 129], ssuf[0:1, 8 * 129:9 * 129],
                                   srw[0:1, 4:5], 0.0, ALU.mult, ALU.add), "s_vb", 1)
            vector.drain()
            for Q in range(5, -1, -1):
                vector.tensor_tensor(ssuf[0:1, 8 * 129:9 * 129], ssuf[0:1, 8 * 129:9 * 129],
                                     ssuf[0:1, (Q + 1) * 129:(Q + 2) * 129], ALU.add)
                vector.drain()
                I(vector.tensor_scalar(ssufb[0:1, Q * 129:(Q + 1) * 129],
                                       ssuf[0:1, 8 * 129:9 * 129],
                                       srw[0:1, 4:5], 0.0, ALU.mult, ALU.add), "s_vb", 1)
                vector.drain()

    head_state = {"av_free": [None] * 4, "bank_free": [None] * 4, "ek": 0,
                  "ctxT_free": [None] * 4}

    def attn_head(hp, h2):
        hh = hp * 2 + h2
        KMs = [32] * 8 if l == 0 else [4 * (Q + 1) for Q in range(8)]
        pre = [0]
        for Q in range(8):
            pre.append(pre[-1] + KMs[Q])
        KMtot = pre[-1]
        ek0 = head_state["ek"]
        ak0 = cnt["s_ak"]
        cv0 = cnt["s_cv"]
        mk0 = cnt["s_mk"]
        vk0 = cnt["s_vk"]
        svbase = h2 * 32 * 130

        def av_inc(Q, kc):
            sfx = Q if l == 1 else 0
            return ak0 + pre[Q] + kc + 1 + sfx

        def emit_av(tensor, Q, kc, KM):
            W(tensor, "s_ek", ek0 + pre[Q] + kc + 1)
            for qt in range(4):
                if kc == 0 and head_state["av_free"][qt] is not None:
                    W(tensor, "s_cv", head_state["av_free"][qt])
                last = (kc == KM - 1) and (l == 0 or Q == 7)
                mm = tensor.matmul(
                    PB[4 + qt][:, 0:129],
                    sexp[:, (kc % 2) * 512 + qt * 128:(kc % 2) * 512 + (qt + 1) * 128],
                    sv[:, (svbase + kc * 130):(svbase + kc * 130 + 129)],
                    start=(kc == 0), stop=last)
            I(mm, "s_ak", 1)

        @blk.tensor
        def _(tensor):
            W(tensor, "s_vb", cnt["s_vb"])
            for Q in range(8):
                KM = KMs[Q]
                for kc in range(KM):
                    bk = kc % 4
                    g = pre[Q] + kc
                    if head_state["bank_free"][bk] is not None:
                        W(tensor, "s_ek", head_state["bank_free"][bk])
                    if head_state["ctxT_free"][bk] is not None:
                        W(tensor, "s_vt", head_state["ctxT_free"][bk])
                        head_state["ctxT_free"][bk] = None
                    head_state["bank_free"][bk] = ek0 + g + 1
                    I(tensor.matmul(PB[bk][:, :],
                                    sk[:, h2 * SEQ + kc * 128:h2 * SEQ + (kc + 1) * 128],
                                    sq[:, h2 * SEQ + Q * 512:h2 * SEQ + (Q + 1) * 512],
                                    start=True, stop=True), "s_mk", 1)
                    if kc >= 1:
                        emit_av(tensor, Q, kc - 1, KM)
                emit_av(tensor, KM and Q, KM - 1, KM)
                if l == 1 and Q < 7:
                    for qt in range(4):
                        mm = tensor.matmul(PB[4 + qt][:, 0:129], sone_rb[:],
                                           ssufb[0:1, Q * 129:(Q + 1) * 129],
                                           start=False, stop=True)
                    I(mm, "s_ak", 1)
                for qt in range(4):
                    head_state["av_free"][qt] = cv0 + Q * 4 + qt + 1

        @blk.scalar
        def _(scalar):
            for Q in range(8):
                KM = KMs[Q]
                for kc in range(KM):
                    g = pre[Q] + kc
                    if g >= 2:
                        gq = g - 2
                        Q2 = 0
                        while pre[Q2 + 1] <= gq:
                            Q2 += 1
                        W(scalar, "s_ak", av_inc(Q2, gq - pre[Q2]))
                    if l == 1 and kc >= 4 * Q:
                        W(scalar, "s_vk", vk0 + 4 * Q + (kc - 4 * Q) + 1)
                    else:
                        W(scalar, "s_mk", mk0 + g + 1)
                    I(scalar.activation(sexp[:, (kc % 2) * 512:(kc % 2 + 1) * 512],
                                        PB[kc % 4][:, :], AF.Exp,
                                        bias=sbias[:, h2:h2 + 1], scale=SC),
                      "s_ek", 1)
        head_state["ek"] = ek0 + KMtot

        def _evac_block(vector, Q):
            KM = KMs[Q]
            for qt in range(4):
                W(vector, "s_ak", av_inc(Q, KM - 1) + (1 if (l == 1 and Q < 7) else 0))
                vector.tensor_copy(szq[:, 0:1], PB[4 + qt][:, 128:129])
                vector.drain()
                vector.reciprocal(szq[:, 1:2], szq[:, 0:1])
                vector.drain()
                I(vector.tensor_scalar(sctx[:, (Q * 4 + qt) * 128:(Q * 4 + qt + 1) * 128],
                                       PB[4 + qt][:, 0:128], szq[:, 1:2], 0.0,
                                       ALU.mult, ALU.add), "s_cv", 1)
                vector.drain()

        @blk.vector
        def _(vector):
            for Q in range(8):
                if Q >= 1:
                    _evac_block(vector, Q - 1)
                if l == 1:
                    for j in range(4):
                        kc = 4 * Q + j
                        g = pre[Q] + kc
                        W(vector, "s_mk", mk0 + g + 1)
                        I(vector.tensor_tensor(PB[kc % 4][:, :], PB[kc % 4][:, :],
                                               sbnd[:, j * 512:(j + 1) * 512],
                                               ALU.mult), "s_vk", 1)
                        vector.drain()
            _evac_block(vector, 7)

        MT0 = cnt["s_mt"]
        VT0 = cnt["s_vt"]

        @blk.tensor
        def _(tensor):
            for i32 in range(32):
                bk = i32 % 4
                W(tensor, "s_cv", cv0 + i32 + 1)
                if head_state["bank_free"][bk] is not None:
                    W(tensor, "s_ek", head_state["bank_free"][bk])
                    head_state["bank_free"][bk] = None
                if i32 >= 4:
                    W(tensor, "s_vt", VT0 + i32 - 3)
                I(tensor.matmul(PB[bk][:, 0:128], sctx[:, i32 * 128:(i32 + 1) * 128],
                                siden[:], start=True, stop=True), "s_mt", 1)

        @blk.vector
        def _(vector):
            for i32 in range(32):
                bk = i32 % 4
                W(vector, "s_mt", MT0 + i32 + 1)
                vector.tensor_copy(sctxT[:, hh * SEQ + i32 * 128:hh * SEQ + (i32 + 1) * 128],
                                   PB[bk][:, 0:128])
                vector.drain()
                I(vector.nop(), "s_vt", 1)
        for bk in range(4):
            head_state["ctxT_free"][bk] = VT0 + 29 + bk

    def proj_phase():
        MP0 = cnt["s_mp"]
        VP0 = cnt["s_vp"]

        @blk.tensor
        def _(tensor):
            g = 0
            for ot in range(16):
                for tb in range(8):
                    bk = g % 8
                    if g >= 8:
                        W(tensor, "s_vp", VP0 + g - 7)
                    elif bk < 4:
                        if head_state["ctxT_free"][bk] is not None:
                            W(tensor, "s_vt", head_state["ctxT_free"][bk])
                            head_state["ctxT_free"][bk] = None
                    else:
                        if head_state["av_free"][bk - 4] is not None:
                            W(tensor, "s_cv", head_state["av_free"][bk - 4])
                            head_state["av_free"][bk - 4] = None
                    for fc in range(4):
                        mm = tensor.matmul(PB[bk][:, :],
                                           swo[:, fc * 2048 + ot * 128:fc * 2048 + (ot + 1) * 128],
                                           sctxT[:, fc * SEQ + tb * 512:fc * SEQ + (tb + 1) * 512],
                                           start=(fc == 0), stop=(fc == 3))
                    I(mm, "s_mp", 1)
                    g += 1

        @blk.vector
        def _(vector):
            for g in range(128):
                bk = g % 8
                W(vector, "s_mp", MP0 + g + 1)
                if g >= 8:
                    pst.wait(vector, g - 8)
                vector.tensor_copy(sfoB[:, bk * 512:(bk + 1) * 512], PB[bk][:, :])
                vector.drain()
                I(vector.nop(), "s_vp", 1)

        @blk.scalar
        def _(scalar):
            for g in range(128):
                ot, tb = g // 8, g % 8
                bk = g % 8
                W(scalar, "s_vp", VP0 + g + 1)
                pst.inc(scalar.dma_start(out=part[128 * ot:128 * (ot + 1), 512 * tb:512 * (tb + 1)],
                                         in_=sfoB[:, bk * 512:(bk + 1) * 512]), g)
            for i in range(8):
                pst.wait(scalar, 120 + i)

    qkv_phase(0)
    for h2 in range(2):
        bounds_head(0, h2)
        if l == 1:
            suffix_head(0, h2)
        attn_head(0, h2)
    qkv_phase(1)
    for h2 in range(2):
        bounds_head(1, h2)
        if l == 1:
            suffix_head(1, h2)
        attn_head(1, h2)
    proj_phase()

    c.close()
    return c


def build_C(with_bias, debug=False):
    c = Ctx("C", debug=debug)
    nc = c.nc
    xT = din(nc, "xT", [D, NT])
    ffn = din(nc, "ffn", [D, NT])
    pstack = din(nc, "pstack", [4 * D, NT])
    if with_bias:
        bias = din(nc, "bias", [D, 1])
    xo = dout(nc, "xo", [D, NT])

    sx = c.sb("sx", [128, 2 * NT])
    sf = c.sb("sf", [128, 2 * NT])
    sp = c.sb("sp", [128, 2 * 4 * NT])
    sb = c.sb("sb", [128, 16]) if with_bias else None
    so = c.sb("so", [128, 2 * NT])

    sems, cnt = {}, {}
    for sn in ["s_b", "s_v", "s_st"]:
        sems[sn] = c.sem(sn)
        cnt[sn] = 0
    px = PDma(c, "px")
    pf = PDma(c, "pf")
    pp = PDma(c, "pp")
    po = PDma(c, "po")

    blk = c.block()

    def I(instr, s, n=1):
        cnt[s] += n
        instr.then_inc(sems[s], n)
        return cnt[s]

    def W(eng, s, v=None):
        eng.wait_ge(sems[s], cnt[s] if v is None else v)

    @blk.sync
    def _(sync):
        if with_bias:
            for j in range(NCH):
                I(sync.dma_start(out=sb[:, j:j + 1], in_=bias[128 * j:128 * (j + 1), :]),
                  "s_b", 16)
        for j in range(NCH):
            if j >= 2:
                W(sync, "s_v", j - 1)
            sl = (j % 2)
            px.inc(sync.dma_start(out=sx[:, sl * NT:(sl + 1) * NT],
                                  in_=xT[128 * j:128 * (j + 1), :]), j)
            pf.inc(sync.dma_start(out=sf[:, sl * NT:(sl + 1) * NT],
                                  in_=ffn[128 * j:128 * (j + 1), :]), j)
            src = pstack.ap().rearrange("(r q p) t -> p r q t", r=4, p=128)[:, :, j, :]
            pp.inc(sync.dma_start(out=sp.ap().rearrange("p (s r t) -> p s r t", s=2, r=4)[:, sl],
                                  in_=src), j)

    @blk.vector
    def _(vector):
        if with_bias:
            W(vector, "s_b", 16 * NCH)
        for j in range(NCH):
            sl = (j % 2)
            px.wait(vector, j)
            pf.wait(vector, j)
            pp.wait(vector, j)
            if j >= 2:
                po.wait(vector, j - 2)
            base = sl * 4 * NT
            vector.tensor_tensor(so[:, sl * NT:(sl + 1) * NT],
                                 sp[:, base:base + NT], sp[:, base + NT:base + 2 * NT],
                                 ALU.add)
            vector.drain()
            vector.tensor_tensor(so[:, sl * NT:(sl + 1) * NT],
                                 so[:, sl * NT:(sl + 1) * NT],
                                 sp[:, base + 2 * NT:base + 3 * NT], ALU.add)
            vector.drain()
            vector.tensor_tensor(so[:, sl * NT:(sl + 1) * NT],
                                 so[:, sl * NT:(sl + 1) * NT],
                                 sp[:, base + 3 * NT:base + 4 * NT], ALU.add)
            vector.drain()
            if with_bias:
                vector.tensor_scalar(so[:, sl * NT:(sl + 1) * NT],
                                     so[:, sl * NT:(sl + 1) * NT],
                                     sb[:, j:j + 1], 0.0, ALU.add, ALU.add)
                vector.drain()
            vector.tensor_tensor(so[:, sl * NT:(sl + 1) * NT],
                                 so[:, sl * NT:(sl + 1) * NT],
                                 sx[:, sl * NT:(sl + 1) * NT], ALU.add)
            vector.drain()
            I(vector.tensor_tensor(so[:, sl * NT:(sl + 1) * NT],
                                   so[:, sl * NT:(sl + 1) * NT],
                                   sf[:, sl * NT:(sl + 1) * NT], ALU.add), "s_v", 1)
            vector.drain()

    @blk.scalar
    def _(scalar):
        for j in range(NCH):
            W(scalar, "s_v", j + 1)
            po.inc(scalar.dma_start(out=xo[128 * j:128 * (j + 1), :],
                                    in_=so[:, (j % 2) * NT:(j % 2 + 1) * NT]), j)
        po.wait(scalar, NCH - 2)
        po.wait(scalar, NCH - 1)

    c.close()
    return c


def build_B0tok(debug=False):
    c = Ctx("B0t", debug=debug)
    nc = c.nc
    SC = 1.0 / np.sqrt(128.0)

    xn = din(nc, "xn", [D, NT], BF16)
    wqkv = din(nc, "wqkv", [D, 3 * D])
    bqkv = din(nc, "bqkv", [1, 3 * D])
    ones_rb = din(nc, "ones_rb", [1, 128], BF16)
    ctx_o = dout(nc, "ctx", [NT, D], BF16)

    sxnb = c.sb("sxnb", [128, NCH * NT], BF16)
    sqkv = c.sb("sqkv", [128, 8 * 6144], BF16)
    swt = c.sb("swt", [128, 2 * 512], BF16)
    sbrf = c.sb("sbrf", [1, 3 * D])
    sbrb = c.sb("sbrb", [1, 3 * D], BF16)
    sone_rb = c.sb("sone_rb", [1, 128], BF16)
    sP = c.sb("sP", [128, 2048], BF16)
    sS = c.sb("sS", [128, 256])
    sE = c.sb("sE", [128, 256])
    sm = c.sb("sm", [128, 1])
    sZ = c.sb("sZ", [128, 32])
    sacc = c.sb("sacc", [128, 2048])
    sco = c.sb("sco", [128, 2 * 2048], BF16)

    PB = [c.ps(f"pb{i}", [128, 512]) for i in range(8)]

    sems, cnt = {}, {}
    for sn in ["s_k", "s_mq", "s_vq", "s_vx", "s_s0", "s_ex", "s_zv", "s_co",
               "s_xa"]:
        sems[sn] = c.sem(sn)
        cnt[sn] = 0
    pwl = PDma(c, "pwl")
    pco = PDma(c, "pco")

    blk = c.block()

    def I(instr, s, n=1):
        cnt[s] += n
        instr.then_inc(sems[s], n)
        return cnt[s]

    def W(eng, s, v=None):
        eng.wait_ge(sems[s], cnt[s] if v is None else v)

    @blk.sync
    def _(sync):
        I(sync.dma_start(out=sbrf[:, :], in_=bqkv[:, :]), "s_k", 16)
        I(sync.dma_start(out=sone_rb[:], in_=ones_rb[:, :]), "s_k", 16)
        for j in range(NCH):
            I(sync.dma_start(out=sxnb[:, j * NT:(j + 1) * NT],
                             in_=xn[128 * j:128 * (j + 1), :]), "s_xa", 16)

    @blk.vector
    def _(vector):
        W(vector, "s_k", 32)
        vector.tensor_copy(sbrb[:, :], sbrf[:, :])
        vector.drain()
        I(vector.nop(), "s_vx", 1)

    def pe_done(n):
        cb, fc = n // 16, n % 16
        return cb * 136 + (fc + 1) * 8

    @blk.gpsimd
    def _(gpsimd):
        for n in range(12 * NCH):
            if n >= 2:
                W(gpsimd, "s_mq", pe_done(n - 2))
            cb, fc = n // 16, n % 16
            pwl.inc(gpsimd.dma_start(
                out=swt[:, (n % 2) * 512:(n % 2 + 1) * 512],
                in_=wqkv[128 * fc:128 * (fc + 1), 512 * cb:512 * (cb + 1)]), n)

    @blk.tensor
    def _(tensor):
        W(tensor, "s_vx", 1)
        W(tensor, "s_xa", 16 * NCH)
        for cb in range(12):
            if cb >= 1:
                W(tensor, "s_vq", cb * 8)
            for fc in range(NCH):
                n = cb * NCH + fc
                pwl.wait(tensor, n)
                sl = (n % 2) * 512
                for tt in range(8):
                    mm = tensor.matmul(PB[tt][:, :],
                                       sxnb[:, fc * NT + tt * 128:fc * NT + (tt + 1) * 128],
                                       swt[:, sl:sl + 512],
                                       start=(fc == 0), stop=False)
                I(mm, "s_mq", 8)
            for tt in range(8):
                mm = tensor.matmul(PB[tt][:, :], sone_rb[:],
                                   sbrb[0:1, 512 * cb:512 * (cb + 1)],
                                   start=False, stop=True)
            I(mm, "s_mq", 8)

    @blk.vector
    def _(vector):
        for cb in range(12):
            W(vector, "s_mq", (cb + 1) * 136)
            for tt in range(8):
                dst = sqkv[:, tt * 6144 + cb * 512:tt * 6144 + (cb + 1) * 512]
                if cb < 4:
                    vector.tensor_scalar(dst, PB[tt][:, :], SC, 0.0, ALU.mult, ALU.add)
                else:
                    vector.tensor_copy(dst, PB[tt][:, :])
                vector.drain()
            I(vector.nop(), "s_vq", 8)

    VQ_ALL = 12 * 8

    @blk.vector
    def _(vector):
        W(vector, "s_vq", VQ_ALL)
        for tt in range(8):
            qb, kb, vb = tt * 6144, tt * 6144 + 2048, tt * 6144 + 4096
            if tt >= 1:
                W(vector, "s_ex", tt)
            for o in range(16):
                lim = (16 - o) * 128
                vector.tensor_tensor(sP[:, 0:lim], sqkv[:, qb:qb + lim],
                                     sqkv[:, kb + o * 128:kb + 2048], ALU.mult)
                vector.drain()
                if o > 0:
                    vector.tensor_tensor(sP[:, lim:2048], sqkv[:, qb + lim:qb + 2048],
                                         sqkv[:, kb:kb + o * 128], ALU.mult)
                    vector.drain()
                vector.reduce_sum(sS[:, o * 16:(o + 1) * 16],
                                  sP.ap().rearrange("p (h e) -> p h e", e=128),
                                  axis=AX.X)
                vector.drain()
            I(vector.reduce_max(sm[:, :], sS[:, :], axis=AX.X, negate=True), "s_s0", 1)
            vector.drain()
            W(vector, "s_ex", tt + 1)
            if tt >= 1:
                W(vector, "s_co", 16 * tt)
            vector.reduce_sum(sZ[:, 0:16],
                              sE.ap().rearrange("p (o h) -> p h o", h=16),
                              axis=AX.X)
            vector.drain()
            vector.reciprocal(sZ[:, 16:32], sZ[:, 0:16])
            vector.drain()
            I(vector.nop(), "s_zv", 1)
            for h in range(16):
                for t in range(16):
                    col = ((t - h) % 16) * 16 + h
                    vsl = sqkv[:, vb + t * 128:vb + (t + 1) * 128]
                    if t == 0:
                        vector.tensor_scalar(sacc[:, h * 128:(h + 1) * 128], vsl,
                                             sE[:, col:col + 1], 0.0, ALU.mult, ALU.add)
                    else:
                        vector.scalar_tensor_tensor(sacc[:, h * 128:(h + 1) * 128], vsl,
                                                    sE[:, col:col + 1],
                                                    sacc[:, h * 128:(h + 1) * 128],
                                                    ALU.mult, ALU.add)
                    vector.drain()
                if tt >= 2:
                    pco.wait(vector, tt - 2)
                I(vector.tensor_scalar(sco[:, (tt % 2) * 2048 + h * 128:(tt % 2) * 2048 + (h + 1) * 128],
                                       sacc[:, h * 128:(h + 1) * 128],
                                       sZ[:, 16 + h:17 + h], 0.0, ALU.mult, ALU.add),
                  "s_co", 1)
                vector.drain()

    @blk.scalar
    def _(scalar):
        for tt in range(8):
            W(scalar, "s_s0", tt + 1)
            if tt >= 1:
                W(scalar, "s_co", 16 * tt)
            I(scalar.activation(sE[:, :], sS[:, :], AF.Exp, bias=sm[:, :]), "s_ex", 1)
            scalar.drain()

    @blk.sync
    def _(sync):
        for tt in range(8):
            W(sync, "s_co", 16 * (tt + 1))
            pco.inc(sync.dma_start(out=ctx_o[128 * tt:128 * (tt + 1), :],
                                   in_=sco[:, (tt % 2) * 2048:(tt % 2 + 1) * 2048]), tt)
        pco.wait(sync, 6)
        pco.wait(sync, 7)

    c.close()
    return c


def build_C0(debug=False):
    c = Ctx("C0", debug=debug)
    nc = c.nc
    Yt = din(nc, "Yt", [D, NT], BF16)
    wproj = din(nc, "wproj", [D, D])
    bias = din(nc, "bias", [D, 1])
    xT = din(nc, "xT", [D, NT])
    ffn = din(nc, "ffn", [D, NT])
    xo = dout(nc, "xo", [D, NT])

    sy = c.sb("sy", [128, NCH * NT], BF16)
    swt = c.sb("swt", [128, 4 * 128], BF16)
    sb = c.sb("sb", [128, 16])
    sx = c.sb("sx", [128, 2 * NT])
    sf = c.sb("sf", [128, 2 * NT])
    so = c.sb("so", [128, 2 * NT])
    PB = [c.ps(f"pb{i}", [128, 512]) for i in range(8)]

    sems, cnt = {}, {}
    for sn in ["s_k", "s_mp", "s_vp", "s_ya"]:
        sems[sn] = c.sem(sn)
        cnt[sn] = 0
    pwl = PDma(c, "pwl", 4)
    px = PDma(c, "px")
    pf = PDma(c, "pf")
    po = PDma(c, "po")

    blk = c.block()

    def I(instr, s, n=1):
        cnt[s] += n
        instr.then_inc(sems[s], n)
        return cnt[s]

    def W(eng, s, v=None):
        eng.wait_ge(sems[s], cnt[s] if v is None else v)

    @blk.sync
    def _(sync):
        for j in range(NCH):
            I(sync.dma_start(out=sb[:, j:j + 1], in_=bias[128 * j:128 * (j + 1), :]),
              "s_k", 16)
        for j in range(NCH):
            I(sync.dma_start(out=sy[:, j * NT:(j + 1) * NT],
                             in_=Yt[128 * j:128 * (j + 1), :]), "s_ya", 16)
        for ot in range(16):
            if ot >= 2:
                W(sync, "s_vp", 2 * (ot - 1))
            sl = (ot % 2)
            px.inc(sync.dma_start(out=sx[:, sl * NT:(sl + 1) * NT],
                                  in_=xT[128 * ot:128 * (ot + 1), :]), ot)
            pf.inc(sync.dma_start(out=sf[:, sl * NT:(sl + 1) * NT],
                                  in_=ffn[128 * ot:128 * (ot + 1), :]), ot)

    @blk.gpsimd
    def _(gpsimd):
        for n in range(16 * 16):
            ot, fc = n // 16, n % 16
            if n >= 4:
                W(gpsimd, "s_mp", n - 3)
            pwl.inc(gpsimd.dma_start(
                out=swt[:, (n % 4) * 128:(n % 4 + 1) * 128],
                in_=wproj[128 * fc:128 * (fc + 1), 128 * ot:128 * (ot + 1)]), n)

    @blk.tensor
    def _(tensor):
        W(tensor, "s_ya", 16 * NCH)
        for ot in range(16):
            b0 = (ot % 4) * 2
            if ot >= 4:
                W(tensor, "s_vp", 2 * (ot - 3))
            for fc in range(NCH):
                n = ot * 16 + fc
                pwl.wait(tensor, n)
                sl = (n % 4) * 128
                for hf in range(2):
                    mm = tensor.matmul(PB[b0 + hf][:, :], swt[:, sl:sl + 128],
                                       sy[:, fc * NT + hf * 512:fc * NT + (hf + 1) * 512],
                                       start=(fc == 0), stop=(fc == NCH - 1))
                I(mm, "s_mp", 1)

    @blk.vector
    def _(vector):
        W(vector, "s_k", 16 * NCH)
        for ot in range(16):
            b0 = (ot % 4) * 2
            W(vector, "s_mp", (ot + 1) * NCH)
            px.wait(vector, ot)
            pf.wait(vector, ot)
            if ot >= 2:
                po.wait(vector, ot - 2)
            sl = (ot % 2)
            for hf in range(2):
                rng = slice(sl * NT + hf * 512, sl * NT + (hf + 1) * 512)
                vector.scalar_tensor_tensor(so[:, rng], PB[b0 + hf][:, :],
                                            sb[:, ot:ot + 1], sx[:, rng],
                                            ALU.add, ALU.add)
                vector.drain()
                vector.tensor_tensor(so[:, rng], so[:, rng], sf[:, rng], ALU.add)
                vector.drain()
            I(vector.nop(), "s_vp", 2)

    @blk.scalar
    def _(scalar):
        for ot in range(16):
            W(scalar, "s_vp", 2 * (ot + 1))
            po.inc(scalar.dma_start(out=xo[128 * ot:128 * (ot + 1), :],
                                    in_=so[:, (ot % 2) * NT:(ot % 2 + 1) * NT]), ot)
        po.wait(scalar, 14)
        po.wait(scalar, 15)

    c.close()
    return c


from concourse.bass_utils import run_bass_kernel_spmd

_CACHE = {}


def _get(name, fn, *args):
    if name not in _CACHE:
        _CACHE[name] = fn(*args)
    return _CACHE[name]


def _consts_f():
    return {"ones_f": np.ones((128, 1), np.float32),
            "ones_r": np.ones((1, 128), np.float32),
            "iden_f": np.eye(128, dtype=np.float32)}


def kernel(**inputs):
    inputs = {k: np.asarray(v) for k, v in inputs.items()}
    sh, percore = host_prep(inputs)
    cons = _consts_f()
    consb = {"ones_fb": np.ones((128, 1), bf16),
             "ones_rb": np.ones((1, 128), bf16),
             "iden_b": sh["iden_b"], "iden_f": cons["iden_f"],
             "ones_f": cons["ones_f"], "ones_r": cons["ones_r"]}
    x0 = inputs["L0_attn_x"]

    xs = [np.ascontiguousarray(x0[c // 4, 1024 * (c % 4):1024 * (c % 4 + 1), :].T)
          .astype(np.float32) for c in range(8)]

    def runA(l, xs_in):
        nc = _get(f"A{l}", build_A, l).nc
        wk = (["w1t", "w2t"] if l == 0 else ["gatet", "upt", "downt"])
        wsrc = (["l0_w1t", "l0_w2t"] if l == 0 else ["l1_gatet", "l1_upt", "l1_downt"])
        shk = (["sh_w1", "sh_w2"] if l == 0 else ["sh_g", "sh_u", "sh_d"])
        shsrc = (["sh_l0w1", "sh_l0w2"] if l == 0 else ["sh_l1g", "sh_l1u", "sh_l1d"])
        ims = []
        for cc in range(8):
            im = {"xT": xs_in[cc], **cons}
            for a, b in zip(wk, wsrc):
                im[a] = sh[b]
            for a, b in zip(shk, shsrc):
                im[a] = percore[cc][b]
            ims.append(im)
        r = run_bass_kernel_spmd(nc, ims, core_ids=list(range(8)))
        return ([np.asarray(r.results[i]["xn"]) for i in range(8)],
                [np.asarray(r.results[i]["ffn"]) for i in range(8)])

    def runB(l, xns):
        nc = _get(f"B{l}", build_B, l).nc
        ims = []
        for cc in range(8):
            s_ = cc // 4
            xnfull = np.concatenate([xns[4 * s_ + r] for r in range(4)], axis=1)
            im = {"xnfull": np.ascontiguousarray(xnfull), **consb}
            if l == 0:
                im["wqkv_sl"] = percore[cc]["l0_wqkv_sl"]
                im["bqkv_sl"] = percore[cc]["l0_bqkv_sl"]
                im["wproj_sl"] = percore[cc]["l0_wproj_sl"]
            else:
                im["wq_sl"] = percore[cc]["l1_wq_sl"]
                im["wk_sl"] = percore[cc]["l1_wk_sl"]
                im["wv_sl"] = percore[cc]["l1_wv_sl"]
                im["wo_sl"] = percore[cc]["l1_wo_sl"]
                im["bands"] = sh["bands"]
            ims.append(im)
        r = run_bass_kernel_spmd(nc, ims, core_ids=list(range(8)))
        return [np.asarray(r.results[i]["part"]) for i in range(8)]

    def runC(l, xs_in, ffns, parts):
        nc = _get(f"C{l==0}", build_C, l == 0).nc
        ims = []
        for cc in range(8):
            s_, r_ = cc // 4, cc % 4
            pstack = np.concatenate(
                [parts[4 * s_ + k][:, 1024 * r_:1024 * (r_ + 1)] for k in range(4)],
                axis=0)
            im = {"xT": xs_in[cc], "ffn": ffns[cc],
                  "pstack": np.ascontiguousarray(pstack)}
            if l == 0:
                im["bias"] = sh["l0_bproj"]
            ims.append(im)
        r = run_bass_kernel_spmd(nc, ims, core_ids=list(range(8)))
        return [np.asarray(r.results[i]["xo"]) for i in range(8)]

    def runB0(xns):
        nc = _get("B0t", build_B0tok).nc
        ims = []
        for cc in range(8):
            ims.append({"xn": xns[cc], "wqkv": inputs["L0_attn_Wqkv"],
                        "bqkv": np.ascontiguousarray(inputs["L0_attn_bqkv"].reshape(1, -1)),
                        "ones_rb": consb["ones_rb"]})
        r = run_bass_kernel_spmd(nc, ims, core_ids=list(range(8)))
        return [np.asarray(r.results[i]["ctx"]) for i in range(8)]

    def runC0(xs_in, ffns, ctxs):
        nc = _get("C0", build_C0).nc
        ims = []
        for cc in range(8):
            s_, r_ = cc // 4, cc % 4
            ctxfull = np.concatenate([ctxs[4 * s_ + kk] for kk in range(4)], axis=0)
            Yb = [np.ascontiguousarray(ctxfull[:, h * 128:(h + 1) * 128]).reshape(256, 2048)
                  for h in range(4 * r_, 4 * r_ + 4)]
            Yt = np.ascontiguousarray(np.concatenate(Yb, axis=0).T)
            ims.append({"Yt": Yt, "wproj": inputs["L0_attn_Wproj"],
                        "bias": sh["l0_bproj"], "xT": xs_in[cc], "ffn": ffns[cc]})
        r = run_bass_kernel_spmd(nc, ims, core_ids=list(range(8)))
        return [np.asarray(r.results[i]["xo"]) for i in range(8)]

    xn0, ffn0 = runA(0, xs)
    ctx0 = runB0(xn0)
    x1 = runC0(xs, ffn0, ctx0)
    xn1, ffn1 = runA(1, x1)
    part1 = runB(1, xn1)
    x2 = runC(1, x1, ffn1, part1)

    out = np.empty((2, 4096, 2048), np.float32)
    for cc in range(8):
        s_, r_ = cc // 4, cc % 4
        out[s_, 1024 * r_:1024 * (r_ + 1), :] = x2[cc].T
    return out
